# version 02 (final)
# baseline (speedup 1.0000x reference)
import math
import contextlib
import numpy as np
import concourse.bass as bass
import concourse.mybir as mybir
from concourse.alu_op_type import AluOpType as ALU
from concourse import library_config as libcfg

F32 = mybir.dt.float32
BF16 = mybir.dt.bfloat16
I16 = mybir.dt.int16
I32 = mybir.dt.int32
AF = mybir.ActivationFunctionType
AX = mybir.AxisListType.X

T = 2048
D = 1024
H = 16
DH = 64
E = 16
CAP = 2560
CBLK = 2048
NCHUNK = T // 128
XROWS = 6656
DROWS = 16 * 2048
NDC = D // 128

ENGINES = ("sync", "vector", "scalar", "tensor", "gpsimd")


class Sem:
    __slots__ = ("h", "n")
    def __init__(self, es, nc, name):
        self.h = es.enter_context(nc.semaphore(name))
        self.n = 0


class Builder:
    def __init__(self, debug_outs=(), stage="full"):
        self.debug_outs = set(debug_outs)
        self.stage = stage
        self.streams = {e: [] for e in ENGINES}
        self.dbg = {}

    def em(self, eng, fn):
        self.streams[eng].append(fn)

    def build(self):
        nc = bass.Bass("TRN2", target_bir_lowering=False)
        self.nc = nc
        es = contextlib.ExitStack()
        self.es = es
        with es:
            self.declare_io()
            self.alloc()
            self.emit_all()
            block = es.enter_context(nc.Block())
            for name in ENGINES:
                eng_dec = getattr(block, name)
                stream = self.streams[name]
                def make(stream):
                    def run(e):
                        for fn in stream:
                            fn(e)
                    return run
                eng_dec(make(stream))
        mybir.codegen_inst_isa_subclasses(nc)
        return nc

    def alias(self, name, shape, dtype, base, off_bytes=0):
        baddr = self.nc.lookup_mloc(base).addr + off_bytes
        return self.nc.alloc_sbuf_tensor_at(name, shape, dtype, offset=baddr)

    def dram_out(self, name, shape, dt):
        return self.nc.dram_tensor(name, shape, dt, kind="ExternalOutput").ap()

    def dbg_out(self, name, shape, dt):
        self.dbg[name] = self.dram_out("dbg_" + name, shape, dt)
        return self.dbg[name]

    def declare_io(self):
        nc = self.nc
        inp = lambda n, s, dt=F32: nc.dram_tensor(n, s, dt, kind="ExternalInput").ap()
        self.x = inp("x", [T, D])
        self.w_in = inp("w_in", [3 * D, D])
        self.b_in = inp("b_in", [3 * D, 1])
        self.w_out = inp("w_out", [D, D])
        self.b_out = inp("b_out", [D, 1])
        self.w_r = inp("w_r", [D, E])
        self.w_g = inp("w_g", [E, D, D])
        self.w_u = inp("w_u", [E, D, D])
        self.w_d = inp("w_d", [E, D, D])
        self.ident = inp("ident", [128, 128], BF16)
        self.identf = inp("identf", [128, 128], F32)
        self.tri = inp("tri", [128, 128], BF16)
        self.iota16 = inp("iota16", [128, E])
        self.iota128 = inp("iota128", [128, 1])
        self.rankmask = inp("rankmask", [8, 1])
        self.masters = inp("masters", [H, 128, 4096], BF16)
        self.identidx_in = inp("identidx_c", [128, 128], F32)
        self.out = self.dram_out("out", [T, D], F32)
        self.xn_hbm = nc.dram_tensor("xn_hbm", [T, D], BF16, kind="Internal").ap()
        self.xe_hbm = nc.dram_tensor("xe_hbm", [XROWS, D], BF16, kind="Internal").ap()
        self.down_hbm = nc.dram_tensor("down_hbm", [DROWS, D], F32, kind="Internal").ap()
        self.cc_in = nc.dram_tensor("cc_in", [1, E], F32, kind="Internal").ap()
        self.cc_out = nc.dram_tensor("cc_out", [8, E], F32, kind="Internal").ap()

    def alloc(self):
        nc, es = self.nc, self.es
        sb = lambda name, shape, dt=F32: es.enter_context(nc.sbuf_tensor(name, shape, dt))
        self.sb = sb
        ps = lambda name, shape, dt=F32: es.enter_context(nc.psum_tensor(name, shape, dt))
        self.pb = [ps(f"pb{i}", [128, 1024], BF16) for i in range(2)]
        self.pf = [ps(f"pf{i}", [128, 512]) for i in range(6)]
        self.xt = [sb(f"xt{c}", [128, D]) for c in range(NCHUNK)]
        self.xnb_all = sb("xnb_all", [128, NCHUNK, D], BF16)
        self.xnb = [self.xnb_all[:, c, :] for c in range(NCHUNK)]
        self.xnT = [sb(f"xnT{d}", [128, T], BF16) for d in range(NDC)]
        self.identt = sb("identt", [128, 128], BF16)
        self.identft = sb("identft", [128, 128], F32)
        self.trit = sb("trit", [128, 128], BF16)
        self.iota16t = sb("iota16t", [128, E])
        self.iota128t = sb("iota128t", [128, 1])
        self.rankmaskt = sb("rankmaskt", [8, 1])
        self.ones1_128_f = sb("ones1_128_f", [1, 128], F32)
        self.onescol_bf = sb("onescol_bf", [128, 1], BF16)
        self.ln = {k: sb("ln_" + k, [128, 1]) for k in ("sum", "sq", "m", "t", "t2", "r", "nm")}
        self.w_rt = sb("w_rt", [128, NDC * E])
        self.xtrT = [sb(f"xtrT{i}", [128, 128]) for i in range(2)]
        self.lg = sb("lg", [128, E])
        self.rt = {k: sb("rt_" + k, [128, 1]) for k in ("m1", "neg", "Z", "t1", "t2")}
        self.rt_mask = [sb(f"rt_mask{k}", [128, E]) for k in range(2)]
        self.rt_tmpA = sb("rt_tmpA", [128, E])
        self.rt_tmpB = sb("rt_tmpB", [128, E])
        self.rt_exps = sb("rt_exps", [128, E])
        self.rt_ab = sb("rt_ab", [128, E], BF16)
        self.rt_R = sb("rt_R", [128, E])
        self.rt_K = sb("rt_K", [128, E])
        self.rt_row = sb("rt_row", [1, E])
        self.rt_e = [sb(f"rt_e{k}", [128, NCHUNK]) for k in range(2)]
        self.rt_w = [sb(f"rt_w{k}", [128, NCHUNK]) for k in range(2)]
        self.rt_Rs = [sb(f"rt_Rs{k}", [128, NCHUNK]) for k in range(2)]
        self.rt_keep = [sb(f"rt_keep{k}", [128, NCHUNK]) for k in range(2)]
        self.rt_Ks = [sb(f"rt_Ks{k}", [128, NCHUNK]) for k in range(2)]
        self.offrow = sb("offrow", [1, E])
        self.koffrow = sb("koffrow", [1, E])
        self.totrow = sb("totrow", [1, E])
        self.hist = sb("hist", [8, E])
        self.thr = sb("thr", [1, E])
        self.thrb = sb("thrb", [128, E])
        self.offb = sb("offb", [128, E])
        self.sl_f32 = [sb(f"sl_f32{k}", [128, NCHUNK]) for k in range(2)]
        self.sl_i32 = [sb(f"sl_i32{k}", [128, NCHUNK], I32) for k in range(2)]
        self.cg_i32 = [sb(f"cg_i32{k}", [128, NCHUNK], I32) for k in range(2)]
        self.cnt_i32 = sb("cnt_i32", [1, E], I32)
        self.identidx = sb("identidx", [128, 128])
        self.gidx16 = [sb(f"gidx16{i}", [128, 128], I16) for i in range(2)]
        self.gidxf = sb("gidxf", [128, 128])
        self.rt_g1 = sb("rt_g1", [128, 128])
        self.rt_tmpC = sb("rt_tmpC", [128, NCHUNK])
        self.kbrow = sb("kbrow", [1, E])
        self.kbrow_i = sb("kbrow_i", [1, E], I32)
        self.kbaseb = sb("kbaseb", [128, E])
        self.cntb = sb("cntb", [128, E])

        self.qkv = [sb(f"qkv{b}", [128, 3, T], BF16) for b in range(2)]
        self.wblk = [sb(f"wblk{b}", [128, 3, 1024], BF16) for b in range(2)]
        self.winT = [self.alias(f"winT{b}", [128, 3, 1024], BF16, self.wblk[b]) for b in range(2)]
        self.masters_sb = sb("masters_sb", [128, 2, 4096], BF16)
        self.master_sb = [self.masters_sb[:, b, :] for b in range(2)]
        self.vt_sb = sb("vt_sb", [128, NCHUNK, 65], BF16)
        self.pt_sb = [sb(f"pt{b}", [128, 512], BF16) for b in range(2)]
        self.rs_sb = sb("rs_sb", [1, 512])
        self.nb_sb = sb("nb_sb", [64, 512])
        self.bi_sb = sb("bi_sb", [128, 24])
        self.bo_row = self.alias("bo_row", [1, D], F32, self.xt[1])
        self.bo_sb = sb("bo_sb", [128, D])
        self.woT = self.alias("woT", [128, NDC, 1024], BF16, self.masters_sb)
        self.wo_blk = sb("wo_blk", [128, D], BF16)
        self.xe_sb = self.xnb_all.ap().rearrange("p c d -> p (c d)").rearrange("p (a t) -> p a t", a=NDC)
        self.wgu = [self.alias(f"wgu{b}", [128, 2, 1024], BF16, self.wblk[b]) for b in range(2)]
        self.wd_sb = [self.alias(f"wd{b}", [128, 1024], BF16, self.wblk[b], 4096) for b in range(2)]
        self.hid_sb = self.alias("hid_sb", [128, 8, 512], BF16, self.masters_sb)
        self.sil_sb = self.alias("sil_sb", [128, 512], F32, self.masters_sb, 8192)
        self.down_sb = [self.alias(f"down_sb{b}", [128, D], F32, self.qkv[0], b * 4096) for b in range(2)]
        self.gbuf = [self.alias(f"gbuf{b}", [128, D], F32, self.qkv[1], b * 4096) for b in range(2)]
        self.xg = [self.alias(f"xg{b}", [128, D], BF16, self.qkv[1], b * 4096) for b in range(2)]
        self.xgidx = [sb(f"xgidx{b}", [128, NCHUNK], I32) for b in range(2)]
        self.tilerow = sb("tilerow", [128, NCHUNK])
        self.wd_extra = self.alias("wd_extra", [128, 2, 1024], BF16, self.qkv[0], 8192)
        self.wk = [sb(f"wk{k}", [128, NCHUNK]) for k in range(2)]
        self.zerotile = self.alias("zerotile", [128, 2048], BF16, self.masters_sb, 0)
        self.zerotile_f = self.alias("zerotile_f", [128, 1024], F32, self.masters_sb, 4096)
        self.zbigf = self.alias("zbigf", [128, 2048], F32, self.masters_sb, 8192)
        self.sq_scr = self.alias("sq_scr", [128, D], F32, self.qkv[0], 0)
        self.dnidx = sb("dnidx", [128, 2], I32)
        self.dnidx_f = sb("dnidx_f", [128, 1])
        self.ceilcnt = sb("ceilcnt", [128, E])
        self.wcol = sb("wcol", [128, 1])
        self.neg8 = sb("neg8", [128, 1])
        self.onesrow_bf = sb("onesrow_bf", [1, 128], BF16)
        self.sqbuf = self.alias("sqbuf", [128, T], BF16, self.xt[0])
        self.lrow = self.alias("lrow", [1, 512], F32, self.xt[2])
        self.negl_bf = self.alias("negl_bf", [1, 512], BF16, self.xt[2], 2048)
        self.epsT = sb("epsT", [1, 1])
        self.mgq = sb("mgq", [1, 2])
        self.mgtmp = sb("mgtmp", [1, 1])
        self.mgv = sb("mgv", [1, 1])
        self.mg8 = sb("mg8", [1, 1])
        self.mgbias = sb("mgbias", [128, 1])

    def emit_all(self):
        es, nc = self.es, self.nc
        self.s = {}
        for name in ("x", "c", "xnf", "xnb", "hbm", "trp", "tre", "tref", "lg",
                     "rt1", "ag", "rt2", "idx", "disp", "z", "xe", "dn", "cg", "dbg", "fin",
                     "w0", "w1", "m0", "m1", "wo", "att", "ao", "moe", "y", "mw0", "mw1", "xr0", "xr1", "dn1", "cg1", "fin1", "xe1"):
            self.s[name] = Sem(es, nc, "s_" + name)
        self.ph_consts()
        self.ph_ln()
        self.ph_transpose()
        self.ph_router1()
        self.ph_allgather()
        self.ph_router2()
        if self.stage == "router":
            self.finish_dbg(); return
        self.ph_idxbuild()
        if self.stage == "idx":
            self.finish_dbg(); return
        self.ph_attn_prep()
        self.ph_attention()
        self.woT_ev = [0] * 8
        self.ph_woT()
        self.ph_outproj()
        if self.stage == "attn":
            self.finish_attn_dbg(); return
        self.ph_moe()
        if self.stage == "moe":
            s = self.s
            s["fin"].n += 16 * NCHUNK
            def syncmo(sy, wy=self.y_attn, wdn=self.moe_done_dn, wdn1=self.moe_done_dn1,
                       base=s["fin"].n - 16 * NCHUNK):
                sy.wait_ge(s["y"].h, wy)
                sy.wait_ge(s["dn"].h, wdn)
                sy.wait_ge(s["dn1"].h, wdn1)
                for c in range(NCHUNK):
                    sy.dma_start(self.out[c * 128:(c + 1) * 128, :], self.xt[c][:, :]).then_inc(s["fin"].h, 16)
                sy.wait_ge(s["fin"].h, base + 16 * NCHUNK)
            self.em("sync", syncmo)
            return
        self.ph_combine()
        if self.stage == "moedbg":
            s = self.s
            o_cnt = self.dbg_out("cnt", [1, E], mybir.dt.int32)
            o_wk = self.dbg_out("wk", [128, 2 * NCHUNK], F32)
            o_cg = self.dbg_out("cg", [128, 2 * NCHUNK], mybir.dt.int32)
            o_dh = self.dbg_out("dh", [E * 128, D], F32)
            o_xt = self.dbg_out("xtd", [T, D], F32)
            s["dbg"].n += 16 * (5 + E + NCHUNK)
            def syncm(sy, wy=s["y"].n, wdn=self.moe_done_dn, wdn1=self.moe_done_dn1,
                      base=s["dbg"].n - 16 * (5 + E + NCHUNK)):
                sy.wait_ge(s["y"].h, wy)
                sy.wait_ge(s["dn"].h, wdn)
                sy.wait_ge(s["dn1"].h, wdn1)
                sy.dma_start(o_cnt[:, :], self.cnt_i32[:, :]).then_inc(s["dbg"].h, 16)
                sy.dma_start(o_wk[:, 0:NCHUNK], self.wk[0][:, :]).then_inc(s["dbg"].h, 16)
                sy.dma_start(o_wk[:, NCHUNK:], self.wk[1][:, :]).then_inc(s["dbg"].h, 16)
                sy.dma_start(o_cg[:, 0:NCHUNK], self.cg_i32[0][:, :]).then_inc(s["dbg"].h, 16)
                sy.dma_start(o_cg[:, NCHUNK:], self.cg_i32[1][:, :]).then_inc(s["dbg"].h, 16)
                for e in range(E):
                    sy.dma_start(o_dh[e * 128:(e + 1) * 128, :], self.down_hbm[e * 2048:e * 2048 + 128, :]).then_inc(s["dbg"].h, 16)
                for c in range(NCHUNK):
                    sy.dma_start(o_xt[c * 128:(c + 1) * 128, :], self.xt[c][:, :]).then_inc(s["dbg"].h, 16)
                sy.wait_ge(s["dbg"].h, base + 16 * (5 + E + NCHUNK))
            self.em("sync", syncm)
        self.ph_final()

    def finish_dbg(self):
        s = self.s
        s["fin"].n += 16 * NCHUNK
        def sync(sy, wx=s["xnf"].n, base=s["fin"].n - 16 * NCHUNK):
            sy.wait_ge(s["xnf"].h, wx)
            for c in range(NCHUNK):
                sy.dma_start(self.out[c * 128:(c + 1) * 128, :], self.xt[c][:, :]).then_inc(s["fin"].h, 16)
            sy.wait_ge(s["fin"].h, base + 16 * NCHUNK)
        self.em("sync", sync)
        s_dbg_final = s["dbg"].n
        def sync2(sy, w=s_dbg_final):
            if w:
                sy.wait_ge(s["dbg"].h, w)
        self.em("sync", sync2)

    def ph_consts(self):
        s = self.s
        def sync0(sy):
            for c in range(NCHUNK):
                sy.dma_start(self.xt[c][:, :], self.x[c * 128:(c + 1) * 128, :]).then_inc(s["x"].h, 16)
            sy.dma_start(self.identt[:], self.ident[:]).then_inc(s["c"].h, 16)
            sy.dma_start(self.identft[:], self.identf[:]).then_inc(s["c"].h, 16)
            sy.dma_start(self.trit[:], self.tri[:]).then_inc(s["c"].h, 16)
            sy.dma_start(self.iota16t[:], self.iota16[:]).then_inc(s["c"].h, 16)
            sy.dma_start(self.iota128t[:], self.iota128[:]).then_inc(s["c"].h, 16)
            sy.dma_start(self.rankmaskt[:], self.rankmask[:]).then_inc(s["c"].h, 16)
            sy.dma_start(self.identidx[:], self.identidx_in[:]).then_inc(s["c"].h, 16)
            for dc in range(NDC):
                sy.dma_start(self.w_rt[:, dc * E:(dc + 1) * E], self.w_r[dc * 128:(dc + 1) * 128, :]).then_inc(s["c"].h, 16)
        s["x"].n += 16 * NCHUNK
        s["c"].n += 16 * (7 + NDC)
        self.em("sync", sync0)

        def vec0(v):
            v.memset(self.ones1_128_f[:], 1.0)
            v.memset(self.onescol_bf[:], 1.0)
            v.memset(self.offrow[:], 0.0)
            v.memset(self.koffrow[:], 0.0)
            v.memset(self.zerotile[:], 0.0)
            v.memset(self.zerotile_f[:], 0.0)
            v.memset(self.zbigf[:], 0.0).then_inc(s["c"].h, 1)
        s["c"].n += 1
        self.em("vector", vec0)
        self.c_all = s["c"].n

        def gps0(g):
            g.load_library(libcfg.mlp)
        self.em("gpsimd", gps0)

        nz_xe = XROWS // 256
        s["z"].n += 16 * nz_xe
        def syncz(sy, wc=self.c_all, nz=nz_xe):
            sy.wait_ge(s["c"].h, wc)
            for i in range(nz):
                sy.dma_start(self.xe_hbm[i * 256:(i + 1) * 256, :].rearrange("(p a) d -> p (a d)", p=128), self.zerotile[:, :]).then_inc(s["z"].h, 16)
        self.em("sync", syncz)
        ndn = DROWS // 256
        s["z"].n += 16 * ndn
        def syncz2(sy, wc=self.c_all, nd=ndn):
            sy.wait_ge(s["c"].h, wc)
            for i in range(nd):
                sy.dma_start(self.down_hbm[i * 256:(i + 1) * 256, :].rearrange("(p a) d -> p (a d)", p=128),
                             self.zbigf[:, :]).then_inc(s["z"].h, 16)
        self.em("sync", syncz2)
        self.z_done = s["z"].n

    def ph_ln(self):
        s = self.s
        prev_ve = 0
        for c in range(NCHUNK):
            xc = self.xt[c]
            xw = 16 * NCHUNK

            sq_done = s["lg"].n + 1; s["lg"].n += 1
            def sca(sc, xc=xc, xw=xw, pv=prev_ve):
                sc.wait_ge(s["x"].h, xw)
                if pv:
                    sc.wait_ge(s["lg"].h, pv)
                sc.activation(self.sq_scr[:], xc[:, :], AF.Square, accum_out=self.ln["sq"][:]).then_inc(s["lg"].h, 1)
            self.em("scalar", sca)

            ve_done = s["lg"].n + 1; s["lg"].n += 1
            prev_ve = ve_done
            def vec(v, xc=xc, xw=xw, sq_done=sq_done):
                v.wait_ge(s["x"].h, xw)
                v.reduce_sum(self.ln["sum"][:], xc[:, :], axis=AX)
                v.drain()
                v.tensor_scalar(self.ln["m"][:], self.ln["sum"][:], 1.0 / D, None, op0=ALU.mult)
                v.drain()
                v.wait_ge(s["lg"].h, sq_done)
                v.tensor_tensor(self.ln["t"][:], self.ln["m"][:], self.ln["m"][:], op=ALU.mult)
                v.drain()
                v.tensor_scalar(self.ln["t2"][:], self.ln["sq"][:], 1.0 / D, None, op0=ALU.mult)
                v.drain()
                v.tensor_tensor(self.ln["t2"][:], self.ln["t2"][:], self.ln["t"][:], op=ALU.subtract)
                v.drain()
                v.tensor_scalar(self.ln["t2"][:], self.ln["t2"][:], 1e-5, None, op0=ALU.add).then_inc(s["lg"].h, 1)
            self.em("vector", vec)

            sq2 = s["lg"].n + 1; s["lg"].n += 1
            def sca2(sc, ve_done=ve_done):
                sc.wait_ge(s["lg"].h, ve_done)
                sc.activation(self.ln["t"][:], self.ln["t2"][:], AF.Sqrt).then_inc(s["lg"].h, 1)
            self.em("scalar", sca2)

            s["xnf"].n += 1
            s["xnb"].n += 1
            def vec3(v, xc=xc, c=c, sq2=sq2):
                v.wait_ge(s["lg"].h, sq2)
                v.reciprocal(self.ln["r"][:], self.ln["t"][:])
                v.drain()
                v.tensor_tensor(self.ln["t"][:], self.ln["r"][:], self.ln["r"][:], op=ALU.mult)
                v.drain()
                v.tensor_tensor(self.ln["t"][:], self.ln["t"][:], self.ln["t2"][:], op=ALU.mult)
                v.drain()
                v.tensor_scalar(self.ln["t"][:], self.ln["t"][:], -0.5, 1.5, op0=ALU.mult, op1=ALU.add)
                v.drain()
                v.tensor_tensor(self.ln["r"][:], self.ln["r"][:], self.ln["t"][:], op=ALU.mult)
                v.drain()
                v.tensor_tensor(self.ln["nm"][:], self.ln["m"][:], self.ln["r"][:], op=ALU.mult)
                v.drain()
                v.tensor_scalar(self.ln["nm"][:], self.ln["nm"][:], -1.0, None, op0=ALU.mult)
                v.drain()
                v.tensor_scalar(xc[:, :], xc[:, :], self.ln["r"][:], self.ln["nm"][:], op0=ALU.mult, op1=ALU.add).then_inc(s["xnf"].h, 1)
                v.drain()
                v.tensor_copy(self.xnb[c], xc[:, :]).then_inc(s["xnb"].h, 1)
            self.em("vector", vec3)

            s["hbm"].n += 16
            def sync(sy, c=c, w=s["xnb"].n):
                sy.wait_ge(s["xnb"].h, w)
                sy.dma_start(self.xn_hbm[c * 128:(c + 1) * 128, :], self.xnb[c]).then_inc(s["hbm"].h, 16)
            self.em("sync", sync)

        if "xn" in self.debug_outs:
            o = self.dbg_out("xn", [T, D], F32)
            s["dbg"].n += 16 * NCHUNK
            def syncd(sy, wx=s["xnf"].n):
                sy.wait_ge(s["xnf"].h, wx)
                for c in range(NCHUNK):
                    sy.dma_start(o[c * 128:(c + 1) * 128, :], self.xt[c][:, :]).then_inc(s["dbg"].h, 16)
            self.em("sync", syncd)

    def ph_transpose(self):
        s = self.s
        gi = 0
        for dc in range(NDC):
            for cg in range(2):
                pp = self.pb[gi % 2]
                trp_t = s["trp"].n + 1; s["trp"].n += 1
                def ten(t, dc=dc, cg=cg, pp=pp, ev=max(0, gi - 1), wxnb=8 * (cg + 1), wc=self.c_all):
                    t.wait_ge(s["xnb"].h, wxnb)
                    t.wait_ge(s["c"].h, wc)
                    if ev:
                        t.wait_ge(s["tre"].h, ev)
                    for i in range(8):
                        c = cg * 8 + i
                        mm = t.transpose(pp[:, i * 128:(i + 1) * 128], self.xnb_all[:, c, dc * 128:(dc + 1) * 128], self.identt[:])
                        if i == 7:
                            mm.then_inc(s["trp"].h, 1)
                self.em("tensor", ten)

                s["tre"].n += 1
                def sca(sc, dc=dc, cg=cg, pp=pp, trp_t=trp_t):
                    sc.wait_ge(s["trp"].h, trp_t)
                    sc.copy(self.xnT[dc][:, cg * 1024:(cg + 1) * 1024], pp[:, :]).then_inc(s["tre"].h, 1)
                self.em("scalar", sca)
                gi += 1
        self.tr_done = s["tre"].n

        if "xnT" in self.debug_outs:
            o = self.dbg_out("xnT", [NDC * 128, T], BF16)
            s["dbg"].n += 16 * NDC
            def syncd(sy, w=s["tre"].n):
                sy.wait_ge(s["tre"].h, w)
                for dc in range(NDC):
                    sy.dma_start(o[dc * 128:(dc + 1) * 128, :], self.xnT[dc][:, :]).then_inc(s["dbg"].h, 16)
            self.em("sync", syncd)

    def ph_router1(self):
        s = self.s
        for c in range(NCHUNK):
            lgv_prev = s["rt1"].n
            for dc in range(NDC):
                pp = self.pf[dc % 2]
                trp_t = s["trp"].n + 1; s["trp"].n += 1
                def ten(t, c=c, dc=dc, pp=pp, wxnf=c + 1, ev=s["tref"].n, wc=self.c_all):
                    t.wait_ge(s["xnf"].h, wxnf)
                    t.wait_ge(s["c"].h, wc)
                    if dc >= 2 or c > 0:
                        t.wait_ge(s["tref"].h, ev)
                    t.transpose(pp[:, 0:128], self.xt[c][:, dc * 128:(dc + 1) * 128], self.identft[:]).then_inc(s["trp"].h, 1)
                self.em("tensor", ten)

                tref_t = s["tref"].n + 1; s["tref"].n += 1
                def vec(v, dc=dc, pp=pp, trp_t=trp_t):
                    v.wait_ge(s["trp"].h, trp_t)
                    v.tensor_copy(self.xtrT[dc % 2][:, :], pp[:, 0:128]).then_inc(s["tref"].h, 1)
                self.em("vector", vec)

                mm_t = s["trp"].n + 1; s["trp"].n += 1
                def ten2(t, c=c, dc=dc, tref_t=tref_t, lgv=self.lg_evac if hasattr(self, "lg_evac") else 0):
                    t.wait_ge(s["tref"].h, tref_t)
                    if dc == 0 and lgv:
                        t.wait_ge(s["rt1"].h, lgv)
                    t.matmul(self.pf[2][:, 0:E], self.xtrT[dc % 2][:, :], self.w_rt[:, dc * E:(dc + 1) * E],
                             start=(dc == 0), stop=(dc == NDC - 1)).then_inc(s["trp"].h, 1)
                self.em("tensor", ten2)
            lg_mm_done = s["trp"].n

            self.lg_evac = s["rt1"].n + 1; s["rt1"].n += 1
            def sca(sc, lg_mm_done=lg_mm_done):
                sc.wait_ge(s["trp"].h, lg_mm_done)
                sc.copy(self.lg[:, :], self.pf[2][:, 0:E]).then_inc(s["rt1"].h, 1)
            self.em("scalar", sca)

            rt1_v = s["rt1"].n + 1; s["rt1"].n += 1
            def vec2(v, c=c, lgv=self.lg_evac):
                v.wait_ge(s["rt1"].h, lgv)
                v.reduce_max(self.rt["m1"][:], self.lg[:, :], axis=AX)
                v.drain()
                v.tensor_scalar(self.rt_mask[0][:, :], self.lg[:, :], self.rt["m1"][:], None, op0=ALU.is_ge)
                v.drain()
                v.tensor_tensor(self.rt_tmpA[:, :], self.iota16t[:, :], self.rt_mask[0][:, :], op=ALU.mult)
                v.drain()
                v.reduce_sum(self.rt_e[0][:, c:c + 1], self.rt_tmpA[:, :], axis=AX)
                v.scalar_tensor_tensor(self.rt_tmpB[:, :], self.rt_mask[0][:, :], -1e30, self.lg[:, :], op0=ALU.mult, op1=ALU.add)
                v.drain()
                v.reduce_max(self.rt["t1"][:], self.rt_tmpB[:, :], axis=AX)
                v.drain()
                v.tensor_scalar(self.rt_mask[1][:, :], self.rt_tmpB[:, :], self.rt["t1"][:], None, op0=ALU.is_ge)
                v.drain()
                v.tensor_tensor(self.rt_tmpA[:, :], self.iota16t[:, :], self.rt_mask[1][:, :], op=ALU.mult)
                v.drain()
                v.reduce_sum(self.rt_e[1][:, c:c + 1], self.rt_tmpA[:, :], axis=AX)
                v.tensor_scalar(self.rt["neg"][:], self.rt["m1"][:], -1.0, None, op0=ALU.mult).then_inc(s["rt1"].h, 1)
            self.em("vector", vec2)

            rt1_e = s["rt1"].n + 1; s["rt1"].n += 1
            def sca2(sc, rt1_v=rt1_v):
                sc.wait_ge(s["rt1"].h, rt1_v)
                sc.activation(self.rt_exps[:, :], self.lg[:, :], AF.Exp, bias=self.rt["neg"][:]).then_inc(s["rt1"].h, 1)
            self.em("scalar", sca2)

            rt1_w = s["rt1"].n + 1; s["rt1"].n += 1
            def vec3(v, c=c, rt1_e=rt1_e):
                v.wait_ge(s["rt1"].h, rt1_e)
                v.reduce_sum(self.rt["Z"][:], self.rt_exps[:, :], axis=AX)
                v.drain()
                v.reciprocal(self.rt["t1"][:], self.rt["Z"][:])
                v.drain()
                v.tensor_copy(self.rt_w[0][:, c:c + 1], self.rt["t1"][:])
                v.tensor_tensor(self.rt_tmpA[:, :], self.rt_exps[:, :], self.rt_mask[1][:, :], op=ALU.mult)
                v.drain()
                v.reduce_sum(self.rt["t2"][:], self.rt_tmpA[:, :], axis=AX)
                v.drain()
                v.tensor_tensor(self.rt_w[1][:, c:c + 1], self.rt["t2"][:], self.rt["t1"][:], op=ALU.mult)
                v.tensor_tensor(self.rt_tmpA[:, :], self.rt_mask[0][:, :], self.rt_mask[1][:, :], op=ALU.add)
                v.drain()
                v.tensor_copy(self.rt_ab[:, :], self.rt_tmpA[:, :]).then_inc(s["rt1"].h, 1)
            self.em("vector", vec3)

            mm_R = s["trp"].n + 1; s["trp"].n += 1
            def ten3(t, rt1_w=rt1_w):
                t.wait_ge(s["rt1"].h, rt1_w)
                t.matmul(self.pf[3][:, 0:E], self.trit[:, :], self.rt_ab[:, :], start=True, stop=True).then_inc(s["trp"].h, 1)
            self.em("tensor", ten3)
            mm_bc = s["trp"].n + 1; s["trp"].n += 1
            def ten4(t, rt1_w=rt1_w):
                t.wait_ge(s["rt1"].h, rt1_w)
                t.matmul(self.pf[4][:, 0:E], self.ones1_128_f[:, :], self.offrow[:, :], start=True, stop=True).then_inc(s["trp"].h, 1)
            self.em("tensor", ten4)
            mm_tot = s["trp"].n + 1; s["trp"].n += 1
            def ten5(t, rt1_w=rt1_w):
                t.wait_ge(s["rt1"].h, rt1_w)
                t.matmul(self.pf[5][0:1, 0:E], self.onescol_bf[:, :], self.rt_ab[:, :], start=True, stop=True).then_inc(s["trp"].h, 1)
            self.em("tensor", ten5)

            rt1_R = s["rt1"].n + 1; s["rt1"].n += 1
            def vec4(v, c=c, mm=max(mm_R, mm_bc, mm_tot)):
                v.wait_ge(s["trp"].h, mm)
                v.tensor_copy(self.offb[:, :], self.pf[4][:, 0:E])
                v.drain()
                v.tensor_tensor(self.rt_R[:, :], self.pf[3][:, 0:E], self.offb[:, :], op=ALU.add)
                v.drain()
                for k in range(2):
                    v.tensor_tensor(self.rt_tmpA[:, :], self.rt_R[:, :], self.rt_mask[k][:, :], op=ALU.mult)
                    v.drain()
                    v.reduce_sum(self.rt_Rs[k][:, c:c + 1], self.rt_tmpA[:, :], axis=AX)
                    v.drain()
                v.tensor_tensor(self.offrow[:, :], self.offrow[:, :], self.pf[5][0:1, 0:E], op=ALU.add).then_inc(s["rt1"].h, 1)
                v.drain()
            self.em("vector", vec4)
        self.rt1_done = s["rt1"].n

        if "route1" in self.debug_outs:
            o1 = self.dbg_out("route1", [128, NCHUNK * 6], F32)
            s["dbg"].n += 16 * 6
            def syncd(sy, w=s["rt1"].n):
                sy.wait_ge(s["rt1"].h, w)
                for i, buf in enumerate([self.rt_e[0], self.rt_e[1], self.rt_w[0], self.rt_w[1], self.rt_Rs[0], self.rt_Rs[1]]):
                    sy.dma_start(o1[:, i * NCHUNK:(i + 1) * NCHUNK], buf[:, :]).then_inc(s["dbg"].h, 16)
            self.em("sync", syncd)
            o2 = self.dbg_out("hist", [1, E], F32)
            s["dbg"].n += 16
            def syncd2(sy, w=s["rt1"].n):
                sy.wait_ge(s["rt1"].h, w)
                sy.dma_start(o2[:, :], self.offrow[:, :]).then_inc(s["dbg"].h, 16)
            self.em("sync", syncd2)

    def ph_allgather(self):
        s = self.s
        s["ag"].n += 16
        def sync(sy, w=self.rt1_done):
            sy.wait_ge(s["rt1"].h, w)
            sy.dma_start(self.cc_in[:, :], self.offrow[:, :]).then_inc(s["ag"].h, 16)
        self.em("sync", sync)

        ag_cc = s["ag"].n + 1; s["ag"].n += 1
        def gps(g, w=s["ag"].n - 1):
            g.wait_ge(s["ag"].h, w)
            g.collective_compute("AllGather", ALU.bypass, replica_groups=[list(range(8))],
                                 ins=[self.cc_in.opt()], outs=[self.cc_out.opt()]).then_inc(s["ag"].h, 1)
        self.em("gpsimd", gps)

        s["ag"].n += 16
        def sync2(sy, w=ag_cc):
            sy.wait_ge(s["ag"].h, w)
            sy.dma_start(self.hist[:, :], self.cc_out[:, :]).then_inc(s["ag"].h, 16)
        self.em("sync", sync2)

        mm_t = s["trp"].n + 1; s["trp"].n += 1
        def ten(t, w=s["ag"].n, wc=self.c_all):
            t.wait_ge(s["ag"].h, w)
            t.wait_ge(s["c"].h, wc)
            t.matmul(self.pf[4][0:1, 0:E], self.rankmaskt[:, :], self.hist[:, :], start=True, stop=True).then_inc(s["trp"].h, 1)
        self.em("tensor", ten)

        ag_thr = s["ag"].n + 1; s["ag"].n += 1
        def vec(v, mm_t=mm_t):
            v.wait_ge(s["trp"].h, mm_t)
            v.tensor_scalar(self.thr[:, :], self.pf[4][0:1, 0:E], -1.0, float(CAP), op0=ALU.mult, op1=ALU.add).then_inc(s["ag"].h, 1)
        self.em("vector", vec)

        mm_t2 = s["trp"].n + 1; s["trp"].n += 1
        def ten2(t, w=ag_thr):
            t.wait_ge(s["ag"].h, w)
            t.matmul(self.pf[4][:, 0:E], self.ones1_128_f[:, :], self.thr[:, :], start=True, stop=True).then_inc(s["trp"].h, 1)
        self.em("tensor", ten2)
        self.ag_final = s["ag"].n + 1; s["ag"].n += 1
        def vec2(v, mm_t2=mm_t2):
            v.wait_ge(s["trp"].h, mm_t2)
            v.tensor_copy(self.thrb[:, :], self.pf[4][:, 0:E]).then_inc(s["ag"].h, 1)
        self.em("vector", vec2)

    def ph_router2(self):
        s = self.s
        for c in range(NCHUNK):
            rt2_a = s["rt2"].n + 1; s["rt2"].n += 1
            def vec(v, c=c, w=self.ag_final):
                v.wait_ge(s["ag"].h, w)
                for k in range(2):
                    v.tensor_scalar(self.rt_mask[k][:, :], self.iota16t[:, :], self.rt_e[k][:, c:c + 1], None, op0=ALU.is_equal)
                    v.drain()
                    v.tensor_tensor(self.rt_tmpA[:, :], self.thrb[:, :], self.rt_mask[k][:, :], op=ALU.mult)
                    v.drain()
                    v.reduce_sum(self.rt["t1"][:], self.rt_tmpA[:, :], axis=AX)
                    v.drain()
                    v.tensor_tensor(self.rt_keep[k][:, c:c + 1], self.rt_Rs[k][:, c:c + 1], self.rt["t1"][:], op=ALU.is_lt)
                    v.drain()
                v.tensor_scalar(self.rt_tmpA[:, :], self.rt_mask[0][:, :], self.rt_keep[0][:, c:c + 1], None, op0=ALU.mult)
                v.drain()
                v.tensor_scalar(self.rt_tmpB[:, :], self.rt_mask[1][:, :], self.rt_keep[1][:, c:c + 1], None, op0=ALU.mult)
                v.drain()
                v.tensor_tensor(self.rt_tmpA[:, :], self.rt_tmpA[:, :], self.rt_tmpB[:, :], op=ALU.add)
                v.drain()
                v.tensor_copy(self.rt_ab[:, :], self.rt_tmpA[:, :]).then_inc(s["rt2"].h, 1)
            self.em("vector", vec)

            mm_K = s["trp"].n + 1; s["trp"].n += 1
            def ten(t, rt2_a=rt2_a):
                t.wait_ge(s["rt2"].h, rt2_a)
                t.matmul(self.pf[3][:, 0:E], self.trit[:, :], self.rt_ab[:, :], start=True, stop=True).then_inc(s["trp"].h, 1)
            self.em("tensor", ten)
            mm_kbc = s["trp"].n + 1; s["trp"].n += 1
            def ten2(t, rt2_a=rt2_a):
                t.wait_ge(s["rt2"].h, rt2_a)
                t.matmul(self.pf[4][:, 0:E], self.ones1_128_f[:, :], self.koffrow[:, :], start=True, stop=True).then_inc(s["trp"].h, 1)
            self.em("tensor", ten2)
            mm_ktot = s["trp"].n + 1; s["trp"].n += 1
            def ten2b(t, rt2_a=rt2_a):
                t.wait_ge(s["rt2"].h, rt2_a)
                t.matmul(self.pf[5][0:1, 0:E], self.onescol_bf[:, :], self.rt_ab[:, :], start=True, stop=True).then_inc(s["trp"].h, 1)
            self.em("tensor", ten2b)

            rt2_b = s["rt2"].n + 1; s["rt2"].n += 1
            def vec2(v, c=c, mm=max(mm_K, mm_kbc, mm_ktot)):
                v.wait_ge(s["trp"].h, mm)
                v.tensor_copy(self.offb[:, :], self.pf[4][:, 0:E])
                v.drain()
                v.tensor_tensor(self.rt_K[:, :], self.pf[3][:, 0:E], self.offb[:, :], op=ALU.add)
                v.drain()
                for k in range(2):
                    v.tensor_tensor(self.rt_tmpA[:, :], self.rt_K[:, :], self.rt_mask[k][:, :], op=ALU.mult)
                    v.drain()
                    v.reduce_sum(self.rt_Ks[k][:, c:c + 1], self.rt_tmpA[:, :], axis=AX)
                    v.drain()
                v.tensor_tensor(self.koffrow[:, :], self.koffrow[:, :], self.pf[5][0:1, 0:E], op=ALU.add).then_inc(s["rt2"].h, 1)
                v.drain()
            self.em("vector", vec2)
        kb_v = s["rt2"].n + 1; s["rt2"].n += 1
        def veck(v, w=self.rt2_done if False else s["rt2"].n - 1):
            v.wait_ge(s["rt2"].h, w)
            v.tensor_copy(self.kbrow_i[:, :], self.koffrow[:, :])
            v.drain()
            v.tensor_scalar(self.kbrow_i[:, :], self.kbrow_i[:, :], 127, None, op0=ALU.add)
            v.drain()
            v.tensor_scalar(self.kbrow_i[:, :], self.kbrow_i[:, :], 7, None, op0=ALU.logical_shift_right)
            v.drain()
            v.tensor_scalar(self.kbrow_i[:, :], self.kbrow_i[:, :], 7, None, op0=ALU.logical_shift_left)
            v.drain()
            v.tensor_copy(self.kbrow[:, :], self.kbrow_i[:, :])
            v.drain()
            for sh in (1, 2, 4, 8):
                v.tensor_tensor(self.kbrow[0:1, sh:E], self.kbrow[0:1, sh:E], self.kbrow[0:1, 0:E - sh], op=ALU.add)
                v.drain()
            v.tensor_copy(self.totrow[:, :], self.kbrow_i[:, :])
            v.drain()
            v.tensor_tensor(self.kbrow[:, :], self.kbrow[:, :], self.totrow[:, :], op=ALU.subtract)
            v.drain()
            v.tensor_copy(self.cnt_i32[:, :], self.koffrow[:, :])
            v.drain()
            v.tensor_copy(self.kbrow_i[:, :], self.kbrow[:, :]).then_inc(s["rt2"].h, 1)
        self.em("vector", veck)

        mm_kb = s["trp"].n + 1; s["trp"].n += 1
        def tenk(t, w=kb_v):
            t.wait_ge(s["rt2"].h, w)
            t.matmul(self.pf[4][:, 0:E], self.ones1_128_f[:, :], self.kbrow[:, :], start=True, stop=True)
            t.matmul(self.pf[5][:, 0:E], self.ones1_128_f[:, :], self.koffrow[:, :], start=True, stop=True)
            t.matmul(self.pf[2][:, 0:E], self.ones1_128_f[:, :], self.totrow[:, :], start=True, stop=True).then_inc(s["trp"].h, 1)
        self.em("tensor", tenk)
        kb_b = s["rt2"].n + 1; s["rt2"].n += 1
        def veck2(v, mm=mm_kb):
            v.wait_ge(s["trp"].h, mm)
            v.tensor_copy(self.kbaseb[:, :], self.pf[4][:, 0:E])
            v.tensor_copy(self.cntb[:, :], self.pf[5][:, 0:E])
            v.tensor_copy(self.ceilcnt[:, :], self.pf[2][:, 0:E]).then_inc(s["rt2"].h, 1)
        self.em("vector", veck2)
        self.rt2_done = s["rt2"].n

        if "route2" in self.debug_outs:
            o1 = self.dbg_out("route2", [128, NCHUNK * 4], F32)
            s["dbg"].n += 16 * 4
            def syncd(sy, w=s["rt2"].n):
                sy.wait_ge(s["rt2"].h, w)
                for i, buf in enumerate([self.rt_keep[0], self.rt_keep[1], self.rt_Ks[0], self.rt_Ks[1]]):
                    sy.dma_start(o1[:, i * NCHUNK:(i + 1) * NCHUNK], buf[:, :]).then_inc(s["dbg"].h, 16)
            self.em("sync", syncd)
            o2 = self.dbg_out("kcnt", [1, E], F32)
            s["dbg"].n += 16
            def syncd2(sy, w=s["rt2"].n):
                sy.wait_ge(s["rt2"].h, w)
                sy.dma_start(o2[:, :], self.koffrow[:, :]).then_inc(s["dbg"].h, 16)
            self.em("sync", syncd2)

    def ph_idxbuild(self):
        s = self.s
        idv = s["idx"].n + 1; s["idx"].n += 1
        def vec(v, w=self.rt2_done):
            v.wait_ge(s["rt2"].h, w)
            for k in range(2):
                for c in range(NCHUNK):
                    v.tensor_scalar(self.rt_mask[0][:, :], self.iota16t[:, :], self.rt_e[k][:, c:c + 1], None, op0=ALU.is_equal)
                    v.drain()
                    v.tensor_tensor(self.rt_tmpA[:, :], self.kbaseb[:, :], self.rt_mask[0][:, :], op=ALU.mult)
                    v.drain()
                    v.reduce_sum(self.rt["t1"][:], self.rt_tmpA[:, :], axis=AX)
                    v.drain()
                    v.tensor_tensor(self.sl_f32[k][:, c:c + 1], self.rt_Ks[k][:, c:c + 1], self.rt["t1"][:], op=ALU.add)
                    v.drain()
                v.tensor_scalar(self.rt_tmpC[:, :], self.rt_e[k][:, :], 2048.0, None, op0=ALU.mult)
                v.drain()
                v.tensor_tensor(self.rt_tmpC[:, :], self.rt_tmpC[:, :], self.rt_Ks[k][:, :], op=ALU.add)
                v.drain()
                v.tensor_tensor(self.rt_tmpC[:, :], self.rt_tmpC[:, :], self.rt_keep[k][:, :], op=ALU.mult)
                v.drain()
                v.tensor_copy(self.cg_i32[k][:, :], self.rt_tmpC[:, :])
                v.drain()
                v.tensor_tensor(self.rt_tmpC[:, :], self.sl_f32[k][:, :], self.rt_keep[k][:, :], op=ALU.mult)
                v.drain()
                v.scalar_tensor_tensor(self.rt_tmpC[:, :], self.rt_keep[k][:, :], -60000.0, self.rt_tmpC[:, :], op0=ALU.mult, op1=ALU.add)
                v.drain()
                v.tensor_scalar(self.rt_tmpC[:, :], self.rt_tmpC[:, :], 60000.0, None, op0=ALU.add)
                v.drain()
                mm = v.tensor_copy(self.sl_i32[k][:, :], self.rt_tmpC[:, :])
                if k == 1:
                    mm.then_inc(s["idx"].h, 1)
                else:
                    v.drain()
        self.em("vector", vec)
        self.sl_ready = idv
        self.cnt_ready = self.rt2_done
        self.cnt_idx_t = idv
        wk_t = s["idx"].n + 1; s["idx"].n += 1
        def vecwk(v, w=self.rt2_done):
            v.wait_ge(s["rt2"].h, w)
            v.tensor_tensor(self.wk[0][:, :], self.rt_w[0][:, :], self.rt_keep[0][:, :], op=ALU.mult)
            v.drain()
            v.tensor_tensor(self.wk[1][:, :], self.rt_w[1][:, :], self.rt_keep[1][:, :], op=ALU.mult).then_inc(s["idx"].h, 1)
        self.em("vector", vecwk)
        self.wk_ready = wk_t

        s["disp"].n += 16 * NCHUNK * 2
        def gps(g, w=idv, wz=self.z_done):
            g.wait_ge(s["idx"].h, w)
            g.wait_ge(s["xnb"].h, NCHUNK)
            g.wait_ge(s["z"].h, wz)
            from concourse.bass import IndirectOffsetOnAxis
            self.gbound = g.to_reg(XROWS - 1)
            bound = self.gbound
            for k in range(2):
                for c in range(NCHUNK):
                    g.indirect_dma_start(
                        self.xe_hbm[:, :],
                        IndirectOffsetOnAxis(ap=self.sl_i32[k][:, c:c + 1], axis=0),
                        self.xnb_all[:, c, :],
                        None,
                        bounds_check=bound,
                        oob_is_err=False,
                    ).then_inc(s["disp"].h, 16)
        self.em("gpsimd", gps)
        self.disp_done = s["disp"].n

        if "idx" in self.debug_outs:
            o1 = self.dbg_out("slots", [128, NCHUNK * 4], F32)
            s["dbg"].n += 16 * 2
            def syncd(sy, w=idv):
                sy.wait_ge(s["idx"].h, w)
                sy.dma_start(o1[:, 0:NCHUNK], self.sl_f32[0][:, :]).then_inc(s["dbg"].h, 16)
                sy.dma_start(o1[:, NCHUNK:2 * NCHUNK], self.sl_f32[1][:, :]).then_inc(s["dbg"].h, 16)
            self.em("sync", syncd)
            o3 = self.dbg_out("cnt", [1, 2 * E], F32)
            s["dbg"].n += 16 * 2
            def syncd3(sy, w=self.rt2_done):
                sy.wait_ge(s["rt2"].h, w)
                sy.dma_start(o3[:, 0:E], self.koffrow[:, :]).then_inc(s["dbg"].h, 16)
                sy.dma_start(o3[:, E:2 * E], self.kbrow[:, :]).then_inc(s["dbg"].h, 16)
            self.em("sync", syncd3)
            o2 = self.dbg_out("xeg", [4 * 128, NDC * 2048], BF16)
            xe_sb = self.sb("xe_dbg", [128, NDC, 2048], BF16)
            zc = s["c"].n + 1; s["c"].n += 1
            def vecz(v):
                v.memset(xe_sb[:, :, :], 0.0).then_inc(s["c"].h, 1)
            self.em("vector", vecz)
            prev_gather = [0, 0]
            for e in range(4):
                gi = self.emit_gather_idx(e, guard=("xe", prev_gather[e % 2]))
                gw = s["xe"].n
                prev_gather[e % 2] = gw + 16
                s["xe"].n += 16
                s["dbg"].n += 16
                dbgtgt = s["dbg"].n
                def gpsd(g, e=e, gi=gi, w=self.disp_done, wc=zc, gw=gw, dt_=dbgtgt):
                    g.wait_ge(s["disp"].h, w)
                    g.wait_ge(s["c"].h, wc)
                    g.wait_ge(s["idx"].h, gi)
                    rv = g.to_reg(g.value_load(self.cnt_i32[0:1, e:e + 1]))
                    g.dma_gather(xe_sb[:, :, :], self.xe_hbm[:, :], self.gidx16[e % 2][:, :],
                                 2048, rv, D, transpose=True).then_inc(s["xe"].h, 16)
                    g.wait_ge(s["dbg"].h, dt_)
                def syncg(sy, e=e, gw=gw):
                    sy.wait_ge(s["xe"].h, gw + 16)
                    sy.dma_start(o2[e * 128:(e + 1) * 128, :], xe_sb.ap().rearrange("p a b -> p (a b)")).then_inc(s["dbg"].h, 16)
                self.em("gpsimd", gpsd)
                self.em("sync", syncg)

    def emit_gather_idx(self, e, guard=None):
        s = self.s
        tgt = s["idx"].n + 1; s["idx"].n += 1
        def vecg(v, e=e, w=self.rt2_done, guard=guard):
            v.wait_ge(s["rt2"].h, w)
            if guard is not None and guard[1]:
                v.wait_ge(s[guard[0]].h, guard[1])
            v.tensor_scalar(self.gidxf[:, :], self.identidx[:, :], self.cntb[:, e:e + 1], None, op0=ALU.is_lt)
            v.drain()
            v.tensor_scalar(self.rt_g1[:, :], self.identidx[:, :], self.kbaseb[:, e:e + 1], 1.0, op0=ALU.add, op1=ALU.add)
            v.drain()
            v.tensor_tensor(self.rt_g1[:, :], self.rt_g1[:, :], self.gidxf[:, :], op=ALU.mult)
            v.drain()
            v.tensor_scalar(self.rt_g1[:, :], self.rt_g1[:, :], -1.0, None, op0=ALU.add)
            v.drain()
            v.tensor_copy(self.gidx16[e % 2][:, :], self.rt_g1[:, :]).then_inc(s["idx"].h, 1)
        self.em("vector", vecg)
        return tgt



def _ph_attn_prep(self):
    s = self.s
    s["c"].n += 16
    def sync(sy):
        sy.dma_start(self.bi_sb[:, :], self.b_in.rearrange("(a p) one -> p (a one)", p=128),
                     allow_slow_non_contiguous=True).then_inc(s["c"].h, 16)
    self.em("sync", sync)
    s["c"].n += 16
    def sync2(sy, wrt=self.rt2_done):
        sy.wait_ge(s["rt2"].h, wrt)
        sy.dma_start(self.bo_row[:, :], self.b_out.rearrange("(one d) x -> one (d x)", one=1)).then_inc(s["c"].h, 16)
    self.em("sync", sync2)
    cw = s["c"].n
    mm_bo = s["trp"].n + 1; s["trp"].n += 1
    def ten(t, w=cw, rt=self.rt2_done):
        t.wait_ge(s["c"].h, w)
        t.wait_ge(s["rt2"].h, rt)
        t.matmul(self.pf[2][:, 0:512], self.ones1_128_f[:, :], self.bo_row[:, 0:512], start=True, stop=True)
        t.matmul(self.pf[3][:, 0:512], self.ones1_128_f[:, :], self.bo_row[:, 512:1024], start=True, stop=True).then_inc(s["trp"].h, 1)
    self.em("tensor", ten)
    s["att"].n += 1
    def vec(v, mm=mm_bo, wz=self.z_done):
        v.wait_ge(s["trp"].h, mm)
        v.wait_ge(s["z"].h, wz)
        v.tensor_copy(self.bo_sb[:, 0:512], self.pf[2][:, 0:512])
        v.tensor_copy(self.bo_sb[:, 512:1024], self.pf[3][:, 0:512])
        v.memset(self.vt_sb[:, :, 64:65], 1.0)
        v.memset(self.neg8[:], -8.0)
        v.memset(self.onesrow_bf[:], 1.0)
        v.memset(self.epsT[:], 1e-30).then_inc(s["att"].h, 1)
    self.em("vector", vec)
    self.att_pre = s["att"].n


def _ph_attention(self):
    s = self.s
    NPAIR = H // 2
    pair_w_done = [0] * NPAIR
    pair_proj_done = [0] * NPAIR
    self._nb_evac = 0
    self._fin_prev = 0
    head_master = [0] * H

    def emit_pair_prep(ph):
        b = ph % 2
        sw = s[f"w{b}"]
        sw.n += 16 * 3
        wld = sw.n
        def gpsw(g, ph=ph, b=b, prevT=pair_proj_done[ph - 2] if ph >= 2 else 0, pre=self.att_pre):
            g.wait_ge(s["att"].h, pre)
            if prevT:
                g.wait_ge(s["att"].h, prevT)
            for i in range(3):
                rows = i * D + ph * 128
                g.dma_start(self.wblk[b][:, i, :], self.w_in[rows:rows + 128, :]).then_inc(s[f"w{b}"].h, 16)
        self.em("gpsimd", gpsw)
        for i in range(3):
            trp_t = s["trp"].n + 1; s["trp"].n += 1
            def ten(t, b=b, i=i, w=wld, ev=s["att"].n):
                t.wait_ge(s[f"w{b}"].h, w)
                if ev:
                    t.wait_ge(s["att"].h, ev)
                for dc in range(NDC):
                    mm = t.transpose(self.pb[0][:, dc * 128:(dc + 1) * 128],
                                     self.wblk[b][:, i, dc * 128:(dc + 1) * 128], self.identt[:])
                    if dc == NDC - 1:
                        mm.then_inc(s["trp"].h, 1)
            self.em("tensor", ten)
            s["att"].n += 1
            def sca(sc, b=b, i=i, w=trp_t):
                sc.wait_ge(s["trp"].h, w)
                sc.copy(self.winT[b][:, i, :], self.pb[0][:, 0:1024]).then_inc(s["att"].h, 1)
            self.em("scalar", sca)
        pair_w_done[ph] = s["att"].n

    def emit_pair_proj(ph):
        b = ph % 2
        for qg in range(4):
            mm_t = s["trp"].n + 1; s["trp"].n += 1
            def ten(t, ph=ph, b=b, qg=qg, w=pair_w_done[ph], tr=self.tr_done, ev=s["att"].n,
                    qdbg=getattr(self, "_qkvdbg", 0) if ph == 2 else 0):
                t.wait_ge(s["att"].h, max(w, ev))
                t.wait_ge(s["tre"].h, tr)
                if qdbg:
                    t.wait_ge(s["dbg"].h, qdbg)
                for i in range(3):
                    for dc in range(NDC):
                        mm = t.matmul(self.pf[3 + i][:, :],
                                      self.winT[b][:, i, dc * 128:(dc + 1) * 128],
                                      self.xnT[dc][:, qg * 512:(qg + 1) * 512],
                                      start=(dc == 0), stop=(dc == NDC - 1))
                        if i == 2 and dc == NDC - 1:
                            mm.then_inc(s["trp"].h, 1)
            self.em("tensor", ten)
            s["att"].n += 1
            def sca(sc, ph=ph, b=b, qg=qg, w=mm_t):
                sc.wait_ge(s["trp"].h, w)
                for i in range(3):
                    mm = sc.activation(self.qkv[b][:, i, qg * 512:(qg + 1) * 512], self.pf[3 + i][:, :],
                                       AF.Identity, bias=self.bi_sb[:, i * 8 + ph:i * 8 + ph + 1])
                    if i == 2:
                        mm.then_inc(s["att"].h, 1)
            self.em("scalar", sca)
        proj_ev = s["att"].n
        mg_start = s["att"].n + 1; s["att"].n += 1
        def vmg0(v, w=proj_ev):
            v.wait_ge(s["att"].h, w)
            v.memset(self.mgq[:, :], 0.0).then_inc(s["att"].h, 1)
        self.em("vector", vmg0)
        prev_v = mg_start
        for qk in range(2):
            sqv = s["att"].n + 1; s["att"].n += 1
            def vsq(v, b=b, qk=qk, w=prev_v):
                v.wait_ge(s["att"].h, w)
                v.tensor_tensor(self.sqbuf[:, :], self.qkv[b][:, qk, :], self.qkv[b][:, qk, :], op=ALU.mult).then_inc(s["att"].h, 1)
            self.em("vector", vsq)
            prev_v = sqv
            for qq in range(4):
                mm_n = s["trp"].n + 1; s["trp"].n += 1
                def tnn(t, qq=qq, w=sqv, wpv=prev_v):
                    t.wait_ge(s["att"].h, max(w, wpv))
                    t.matmul(self.pf[5][0:1, :], self.onescol_bf[:, :],
                             self.sqbuf[:, qq * 512:(qq + 1) * 512], start=True, stop=True).then_inc(s["trp"].h, 1)
                self.em("tensor", tnn)
                rv = s["att"].n + 1; s["att"].n += 1
                def vred(v, qk=qk, w=mm_n):
                    v.wait_ge(s["trp"].h, w)
                    v.reduce_max(self.mgtmp[:, :], self.pf[5][0:1, :], axis=AX)
                    v.drain()
                    v.tensor_tensor(self.mgq[:, qk:qk + 1], self.mgq[:, qk:qk + 1], self.mgtmp[:, :], op=ALU.max).then_inc(s["att"].h, 1)
                    v.drain()
                self.em("vector", vred)
                prev_v = rv
        mg_mul = s["att"].n + 1; s["att"].n += 1
        def vmg1(v, w=prev_v):
            v.wait_ge(s["att"].h, w)
            v.tensor_tensor(self.mgtmp[:, :], self.mgq[:, 0:1], self.mgq[:, 1:2], op=ALU.mult).then_inc(s["att"].h, 1)
            v.drain()
        self.em("vector", vmg1)
        mg_sq = s["att"].n + 1; s["att"].n += 1
        def smg(sc, w=mg_mul):
            sc.wait_ge(s["att"].h, w)
            sc.activation(self.mgtmp[:, :], self.mgtmp[:, :], AF.Sqrt).then_inc(s["att"].h, 1)
        self.em("scalar", smg)
        mg_fin = s["att"].n + 1; s["att"].n += 1
        def vmg2(v, w=mg_sq):
            v.wait_ge(s["att"].h, w)
            v.tensor_scalar(self.mgv[:, :], self.mgtmp[:, :], 0.125, 10.0, op0=ALU.mult, op1=ALU.add)
            v.drain()
            v.tensor_scalar(self.mg8[:, :], self.mgv[:, :], -8.0, None, op0=ALU.mult).then_inc(s["att"].h, 1)
            v.drain()
        self.em("vector", vmg2)
        mm_bc = s["trp"].n + 1; s["trp"].n += 1
        def tbc(t, w=mg_fin):
            t.wait_ge(s["att"].h, w)
            t.matmul(self.pf[5][0:128, 0:1], self.ones1_128_f[:, :], self.mgv[:, :], start=True, stop=True).then_inc(s["trp"].h, 1)
        self.em("tensor", tbc)
        mg_done = s["att"].n + 1; s["att"].n += 1
        def vmg3(v, w=mm_bc):
            v.wait_ge(s["trp"].h, w)
            v.tensor_scalar(self.mgbias[:, :], self.pf[5][0:128, 0:1], -0.125, None, op0=ALU.mult).then_inc(s["att"].h, 1)
        self.em("vector", vmg3)
        pair_proj_done[ph] = s["att"].n
        if ph == 0 and "qkv0" in self.debug_outs:
            o = self.dbg_out("qkv0", [128, 3 * T], BF16)
            s["dbg"].n += 16
            def syncq(sy, w=pair_proj_done[0]):
                sy.wait_ge(s["att"].h, w)
                sy.dma_start(o[:, :], self.qkv[0].ap().rearrange("p a b -> p (a b)")).then_inc(s["dbg"].h, 16)
            self.em("sync", syncq)
            self._qkvdbg = s["dbg"].n

    def emit_master_load(h):
        sm = s[f"m{h % 2}"]
        sm.n += 16
        def sync(sy, h=h, prevdone=self.head_done[h - 2] if h >= 2 else 0, wz=self.z_done, pre=self.att_pre):
            sy.wait_ge(s["z"].h, wz)
            sy.wait_ge(s["att"].h, pre)
            if prevdone:
                sy.wait_ge(s["att"].h, prevdone)
            sy.dma_start(self.master_sb[h % 2], self.masters[h, :, :]).then_inc(s[f"m{h % 2}"].h, 16)
        self.em("sync", sync)
        return sm.n

    def emit_head(h):
        ph, hh = h // 2, h % 2
        b = ph % 2
        mw = head_master[h]
        for grp in range(2):
            trp_t = s["trp"].n + 1; s["trp"].n += 1
            def ten(t, b=b, hh=hh, grp=grp, w=pair_proj_done[ph], ev=s["att"].n):
                t.wait_ge(s["att"].h, max(w, ev))
                for i in range(8):
                    kt = grp * 8 + i
                    mm = t.transpose(self.pb[0][:, i * 64:(i + 1) * 64],
                                     self.qkv[b][hh * 64:(hh + 1) * 64, 2, kt * 128:(kt + 1) * 128],
                                     self.identt[hh * 64:(hh + 1) * 64, hh * 64:(hh + 1) * 64])
                    if i == 7:
                        mm.then_inc(s["trp"].h, 1)
            self.em("tensor", ten)
            s["att"].n += 1
            def sca(sc, grp=grp, w=trp_t):
                sc.wait_ge(s["trp"].h, w)
                sc.copy(self.vt_sb[:, grp * 8:(grp + 1) * 8, 0:64],
                        self.pb[0][:, 0:512].rearrange("p (a b) -> p a b", a=8)).then_inc(s["att"].h, 1)
            self.em("scalar", sca)
        vt_done = s["att"].n

        for qg in range(4):
            exp_ts = []
            for kt in range(NCHUNK):
                sb_ = kt % 2
                mm_t = s["trp"].n + 2; s["trp"].n += 2
                def tenA(t, b=b, hh=hh, h=h, qg=qg, kt=kt, sb_=sb_, w=vt_done, mst=mw,
                         exp_prev=exp_ts[kt - 2] if kt >= 2 else 0, finw=self._fin_prev):
                    t.wait_ge(s["att"].h, w)
                    t.wait_ge(s[f"m{h % 2}"].h, mst)
                    if kt >= 2:
                        t.wait_ge(s["att"].h, exp_prev)
                    elif finw:
                        t.wait_ge(s["att"].h, finw)
                    t.matmul(self.pf[sb_][:, :],
                             self.qkv[b][hh * 64:(hh + 1) * 64, 1, kt * 128:(kt + 1) * 128],
                             self.qkv[b][hh * 64:(hh + 1) * 64, 0, qg * 512:(qg + 1) * 512],
                             start=True, stop=False).then_inc(s["trp"].h, 1)
                    off = qg * 512 - kt * 128 + 1920
                    t.matmul(self.pf[sb_][:, :], self.identt[:, :],
                             self.master_sb[h % 2][:, off:off + 512],
                             start=False, stop=True).then_inc(s["trp"].h, 1)
                self.em("tensor", tenA)
                exp_t = s["att"].n + 1; s["att"].n += 1
                exp_ts.append(exp_t)
                def scaA(sc, sb_=sb_, w=mm_t):
                    sc.wait_ge(s["trp"].h, w)
                    sc.activation(self.pt_sb[sb_][:, :], self.pf[sb_][:, :], AF.Exp,
                                  bias=self.mgbias[:, :], scale=1.0 / 64.0).then_inc(s["att"].h, 1)
                self.em("scalar", scaA)
                mm_us = s["trp"].n + 1; s["trp"].n += 1
                def tenUS(t, kt=kt, sb_=sb_, w=exp_t, finw=self._fin_prev):
                    t.wait_ge(s["att"].h, w)
                    if kt == 0 and finw:
                        t.wait_ge(s["att"].h, finw)
                    t.matmul(self.pf[2][0:1, :], self.onescol_bf[:, :], self.pt_sb[sb_][:, :],
                             start=(kt == 0), stop=(kt == NCHUNK - 1)).then_inc(s["trp"].h, 1)
                self.em("tensor", tenUS)
                us_mm_last = mm_us
            ln_t = s["att"].n + 1; s["att"].n += 1
            def scaL(sc, w=us_mm_last):
                sc.wait_ge(s["trp"].h, w)
                sc.activation(self.lrow[:, :], self.pf[2][0:1, :], AF.Ln, bias=self.epsT[:, :]).then_inc(s["att"].h, 1)
            self.em("scalar", scaL)
            ngl_t = s["att"].n + 1; s["att"].n += 1
            def vecL(v, w=ln_t):
                v.wait_ge(s["att"].h, w)
                v.tensor_scalar(self.negl_bf[:, :], self.lrow[:, :], -64.0, self.mg8[:, :], op0=ALU.mult, op1=ALU.add).then_inc(s["att"].h, 1)
                v.drain()
            self.em("vector", vecL)
            for kt in range(NCHUNK):
                sb_ = kt % 2
                mm_t = s["trp"].n + 3; s["trp"].n += 3
                def tenB(t, b=b, hh=hh, h=h, qg=qg, kt=kt, sb_=sb_,
                         exp_prev=exp_ts[NCHUNK + kt - 2] if kt >= 2 else 0, wngl=ngl_t):
                    t.wait_ge(s["att"].h, wngl)
                    if kt >= 2:
                        t.wait_ge(s["att"].h, exp_prev)
                    t.matmul(self.pf[sb_][:, :],
                             self.qkv[b][hh * 64:(hh + 1) * 64, 1, kt * 128:(kt + 1) * 128],
                             self.qkv[b][hh * 64:(hh + 1) * 64, 0, qg * 512:(qg + 1) * 512],
                             start=True, stop=False).then_inc(s["trp"].h, 1)
                    off = qg * 512 - kt * 128 + 1920
                    t.matmul(self.pf[sb_][:, :], self.identt[:, :],
                             self.master_sb[h % 2][:, off:off + 512],
                             start=False, stop=False).then_inc(s["trp"].h, 1)
                    t.matmul(self.pf[sb_][:, :], self.onesrow_bf[:, :], self.negl_bf[:, :],
                             start=False, stop=True).then_inc(s["trp"].h, 1)
                self.em("tensor", tenB)
                exp_t = s["att"].n + 1; s["att"].n += 1
                exp_ts.append(exp_t)
                def scaB(sc, sb_=sb_, w=mm_t):
                    sc.wait_ge(s["trp"].h, w)
                    sc.activation(self.pt_sb[sb_][:, :], self.pf[sb_][:, :], AF.Exp,
                                  scale=0.125).then_inc(s["att"].h, 1)
                self.em("scalar", scaB)
                mm_pv = s["trp"].n + 1; s["trp"].n += 1
                def tenPV(t, kt=kt, sb_=sb_, w=exp_t):
                    t.wait_ge(s["att"].h, w)
                    t.matmul(self.pf[2][0:65, :], self.vt_sb[:, kt, :], self.pt_sb[sb_][:, :],
                             start=(kt == 0), stop=(kt == NCHUNK - 1)).then_inc(s["trp"].h, 1)
                self.em("tensor", tenPV)
                o_mm_last = mm_pv
            rec_t = s["att"].n + 1; s["att"].n += 1
            def vec(v, w=o_mm_last):
                v.wait_ge(s["trp"].h, w)
                v.tensor_scalar(self.rs_sb[:, :], self.pf[2][64:65, :], 1e-33, None, op0=ALU.add)
                v.drain()
                v.reciprocal(self.rs_sb[:, :], self.rs_sb[:, :]).then_inc(s["att"].h, 1)
                v.drain()
            self.em("vector", vec)
            mm_nb = s["trp"].n + 1; s["trp"].n += 1
            def ten3(t, w=rec_t):
                t.wait_ge(s["att"].h, w)
                t.matmul(self.pf[0][0:64, :], self.ones1_128_f[:, 0:64], self.rs_sb[:, :],
                         start=True, stop=True).then_inc(s["trp"].h, 1)
            self.em("tensor", ten3)
            nb_t = s["att"].n + 1; s["att"].n += 1
            def sca2(sc, w=mm_nb):
                sc.wait_ge(s["trp"].h, w)
                sc.copy(self.nb_sb[:, :], self.pf[0][0:64, :]).then_inc(s["att"].h, 1)
            self.em("scalar", sca2)
            self._nb_evac = nb_t
            fin_t = s["att"].n + 1; s["att"].n += 1
            def vec2(v, h=h, qg=qg, w=nb_t, wdisp=self.disp_done, whbm=16 * NCHUNK):
                v.wait_ge(s["att"].h, w)
                v.wait_ge(s["disp"].h, wdisp)
                v.wait_ge(s["hbm"].h, whbm)
                v.tensor_tensor(self.attn_cat_view(h, qg), self.pf[2][0:64, :], self.nb_sb[:, :],
                                op=ALU.mult).then_inc(s["att"].h, 1)
            self.em("vector", vec2)
            self._fin_prev = fin_t
        self.head_done[h] = s["att"].n

    self.head_done = [0] * H
    emit_pair_prep(0)
    head_master[0] = emit_master_load(0)
    head_master[1] = emit_master_load(1)
    emit_pair_proj(0)
    for ph in range(NPAIR):
        if ph + 1 < NPAIR:
            emit_pair_prep(ph + 1)
            emit_pair_proj(ph + 1)
        emit_head(2 * ph)
        emit_head(2 * ph + 1)
        if ph + 1 < NPAIR:
            head_master[2 * ph + 2] = emit_master_load(2 * ph + 2)
            head_master[2 * ph + 3] = emit_master_load(2 * ph + 3)
    self.attn_done = s["att"].n


def _attn_cat_view(self, h, qg):
    dcc, sub = (h * 64) // 128, (h * 64) % 128
    v = self.xnb_all.ap().rearrange("p c d -> p (c d)").rearrange("p (a t) -> p a t", a=NDC)
    return v[sub:sub + 64, dcc, qg * 512:(qg + 1) * 512]


def _ph_woT(self):
    s = self.s
    for jb in range(8):
        s["wo"].n += 16
        wld = s["wo"].n
        def gpsw(g, jb=jb, watt=self.attn_done, prev=self.woT_ev[jb - 1] if jb else 0):
            if jb == 0:
                g.wait_ge(s["att"].h, watt)
            if prev:
                g.wait_ge(s["ao"].h, prev)
            g.dma_start(self.wo_blk[:, :], self.w_out[jb * 128:(jb + 1) * 128, :]).then_inc(s["wo"].h, 16)
        self.em("gpsimd", gpsw)
        trp_t = s["trp"].n + 1; s["trp"].n += 1
        def ten(t, jb=jb, w=wld, watt=self.attn_done, evprev=s["ao"].n):
            t.wait_ge(s["wo"].h, w)
            t.wait_ge(s["att"].h, watt)
            if evprev:
                t.wait_ge(s["ao"].h, evprev)
            for dc in range(NDC):
                mm = t.transpose(self.pb[1][:, dc * 128:(dc + 1) * 128],
                                 self.wo_blk[:, dc * 128:(dc + 1) * 128], self.identt[:])
                if dc == NDC - 1:
                    mm.then_inc(s["trp"].h, 1)
        self.em("tensor", ten)
        s["ao"].n += 1
        self.woT_ev[jb] = s["ao"].n
        def sca(sc, jb=jb, w=trp_t):
            sc.wait_ge(s["trp"].h, w)
            for dc in range(NDC):
                mm = sc.copy(self.woT[:, dc, jb * 128:(jb + 1) * 128], self.pb[1][:, dc * 128:(dc + 1) * 128])
                if dc == NDC - 1:
                    mm.then_inc(s["ao"].h, 1)
        self.em("scalar", sca)
    self.woT_done = s["ao"].n


def _ph_outproj(self):
    s = self.s
    acat = self.xnb_all.ap().rearrange("p c d -> p (c d)").rearrange("p (a t) -> p a t", a=NDC)
    for c in range(NCHUNK):
        mm_t = s["trp"].n + 1; s["trp"].n += 1
        def ten(t, c=c, w=self.attn_done, wwo=self.woT_done, evprev=s["y"].n):
            t.wait_ge(s["att"].h, w)
            t.wait_ge(s["ao"].h, wwo)
            if evprev:
                t.wait_ge(s["y"].h, evprev)
            for half in range(2):
                for dc in range(NDC):
                    mm = t.matmul(self.pf[4 + half][:, :], acat[:, dc, c * 128:(c + 1) * 128],
                                  self.woT[:, dc, half * 512:(half + 1) * 512],
                                  start=(dc == 0), stop=(dc == NDC - 1))
                    if half == 1 and dc == NDC - 1:
                        mm.then_inc(s["trp"].h, 1)
        self.em("tensor", ten)
        s["y"].n += 1
        def vec(v, c=c, w=mm_t):
            v.wait_ge(s["trp"].h, w)
            v.tensor_tensor(self.xt[c][:, 0:512], self.pf[4][:, :], self.bo_sb[:, 0:512], op=ALU.add)
            v.tensor_tensor(self.xt[c][:, 512:1024], self.pf[5][:, :], self.bo_sb[:, 512:1024], op=ALU.add).then_inc(s["y"].h, 1)
        self.em("vector", vec)
    self.y_attn = s["y"].n


def _finish_attn_dbg(self):
    s = self.s
    if "yattn" in self.debug_outs:
        o = self.dbg_out("yattn", [T, D], F32)
        s["dbg"].n += 16 * NCHUNK
        def syncd(sy, w=self.y_attn):
            sy.wait_ge(s["y"].h, w)
            for c in range(NCHUNK):
                sy.dma_start(o[c * 128:(c + 1) * 128, :], self.xt[c][:, :]).then_inc(s["dbg"].h, 16)
        self.em("sync", syncd)
    s["fin"].n += 16 * NCHUNK
    def sync(sy, wy=self.y_attn, base=s["fin"].n - 16 * NCHUNK, dw=s["dbg"].n):
        sy.wait_ge(s["y"].h, wy)
        for c in range(NCHUNK):
            sy.dma_start(self.out[c * 128:(c + 1) * 128, :], self.xt[c][:, :]).then_inc(s["fin"].h, 16)
        sy.wait_ge(s["fin"].h, base + 16 * NCHUNK)
        if dw:
            sy.wait_ge(s["dbg"].h, dw)
    self.em("sync", sync)


def _ph_moe(self):
    s = self.s
    NG = 4
    wg_ar = lambda dc: self.xnT[dc // 2][:, (dc % 2) * 1024:(dc % 2 + 1) * 1024]
    wu_ar = lambda dc: self.xnT[4 + dc // 2][:, (dc % 2) * 1024:(dc % 2 + 1) * 1024]
    def wd_ar(hc):
        if hc < 3:
            return self.wblk[0][:, hc, :]
        if hc < 6:
            return self.wblk[1][:, hc - 3, :]
        return self.wd_extra[:, hc - 6, :]

    xe_users_done = 0
    w_users_done = 0
    prev_group_pf = 0
    self.dn_total = 0
    self.dn_par = [0, 0]
    last_ml = 0
    ms_t = s["moe"].n + 1; s["moe"].n += 1
    def vec_ms(v, wy=self.y_attn, wd=self.disp_done, wh=s["hbm"].n):
        v.wait_ge(s["y"].h, wy)
        v.wait_ge(s["disp"].h, wd)
        v.wait_ge(s["hbm"].h, wh)
        for jt in range(NCHUNK):
            v.tensor_scalar(self.tilerow[:, jt:jt + 1], self.iota128t[:, :], float(128 * jt), None, op0=ALU.add)
        v.drain()
        v.memset(self.xe_sb.rearrange("p a b -> p (a b)"), 0.0)
        v.memset(self.xg[0][:, :], 0.0)
        v.memset(self.xg[1][:, :], 0.0).then_inc(s["moe"].h, 1)
    self.em("vector", vec_ms)
    xe_memset_t = ms_t
    NOW = "noweights" in self.debug_outs
    NOF = "noffn" in self.debug_outs
    xg_free = [0, 0]
    gat_tgt = [0, 0]
    xgidx_free = [(0, 0), (0, 0)]
    from concourse.bass import IndirectOffsetOnAxis
    for e in range(E):
        gi = s["idx"].n + 1; s["idx"].n += 1
        def vec_gi(v, e=e, w=self.rt2_done, wg=xgidx_free[e % 2]):
            v.wait_ge(s["rt2"].h, w)
            if wg[0]:
                v.wait_ge(s["xe"].h, wg[0])
            if wg[1]:
                v.wait_ge(s["xe1"].h, wg[1])
            v.tensor_scalar(self.rt_tmpA[:, :], self.tilerow[:, :], self.cntb[:, e:e + 1], None, op0=ALU.is_lt)
            v.drain()
            v.tensor_scalar(self.rt_tmpB[:, :], self.tilerow[:, :], self.kbaseb[:, e:e + 1], -60000.0, op0=ALU.add, op1=ALU.add)
            v.drain()
            v.tensor_tensor(self.rt_tmpB[:, :], self.rt_tmpB[:, :], self.rt_tmpA[:, :], op=ALU.mult)
            v.drain()
            v.tensor_scalar(self.rt_tmpB[:, :], self.rt_tmpB[:, :], 60000.0, None, op0=ALU.add)
            v.drain()
            v.tensor_copy(self.xgidx[e % 2][:, :], self.rt_tmpB[:, :]).then_inc(s["idx"].h, 1)
        self.em("vector", vec_gi)
        evac_ts = []
        for jt in range(NCHUNK):
            par = jt % 2
            xep = "xe" if par == 0 else "xe1"
            s[xep].n += 16
            gt = s[xep].n
            def gps_g(g, e=e, jt=jt, xep=xep, gi=gi, wdisp=self.disp_done, wz=self.z_done,
                      wms=xe_memset_t, wfree=xg_free[par]):
                g.wait_ge(s["idx"].h, gi)
                g.wait_ge(s["disp"].h, wdisp)
                g.wait_ge(s["z"].h, wz)
                g.wait_ge(s["moe"].h, wms)
                if wfree:
                    g.wait_ge(s["trp"].h, wfree)
                g.indirect_dma_start(self.xg[jt % 2][:, :], None, self.xe_hbm[:, :],
                                     IndirectOffsetOnAxis(ap=self.xgidx[e % 2][:, jt:jt + 1], axis=0),
                                     bounds_check=self.gbound, oob_is_err=False).then_inc(s[xep].h, 16)
            self.em("gpsimd", gps_g)
            gat_tgt[par] = gt
            tr_t = s["trp"].n + 1; s["trp"].n += 1
            def ten_x(t, e=e, jt=jt, par=par, xep=xep, gw=gt, wev=s["moe"].n):
                t.wait_ge(s[xep].h, gw)
                t.wait_ge(s["moe"].h, wev)
                rc = t.to_reg(t.value_load(self.cnt_i32[0:1, e:e + 1]))
                with t.If_cmp(rc, 128 * jt, "IS_GT"):
                    for dc in range(NDC):
                        mm = t.transpose(self.pb[par][:, dc * 128:(dc + 1) * 128],
                                         self.xg[par][:, dc * 128:(dc + 1) * 128], self.identt[:])
                        if dc == NDC - 1:
                            mm.then_inc(s["trp"].h, 1)
                with t.Else():
                    t.sem_inc(s["trp"].h, 1)
                t.end_ifs()
                t.free_register(rc)
            self.em("tensor", ten_x)
            xg_free[par] = tr_t
            ev_t = s["moe"].n + 1; s["moe"].n += 1
            def sca_x(sc, e=e, jt=jt, par=par, w=tr_t, wu=xe_users_done):
                sc.wait_ge(s["trp"].h, w)
                if wu:
                    sc.wait_ge(s["trp"].h, wu)
                rc = sc.to_reg(sc.value_load(self.cnt_i32[0:1, e:e + 1]))
                with sc.If_cmp(rc, 128 * jt, "IS_GT"):
                    for dc in range(NDC):
                        mm = sc.copy(self.xe_sb[:, dc, jt * 128:(jt + 1) * 128],
                                     self.pb[par][:, dc * 128:(dc + 1) * 128])
                        if dc == NDC - 1:
                            mm.then_inc(s["moe"].h, 1)
                with sc.Else():
                    sc.sem_inc(s["moe"].h, 1)
                sc.end_ifs()
                sc.free_register(rc)
            self.em("scalar", sca_x)
            evac_ts.append(ev_t)
        xgidx_free[e % 2] = (gat_tgt[0], gat_tgt[1])
        xe_ready = evac_ts[-1]

        if not NOW:
            s["mw0"].n += 16 * 24
        wld = s["mw0"].n
        def gps_w(g, e=e, wy=self.y_attn, wu=w_users_done):
            g.wait_ge(s["y"].h, wy)
            if wu:
                g.wait_ge(s["trp"].h, wu)
            for dc in range(NDC):
                g.dma_start(wg_ar(dc), self.w_g[e, dc * 128:(dc + 1) * 128, :]).then_inc(s["mw0"].h, 16)
            for dc in range(NDC):
                g.dma_start(wu_ar(dc), self.w_u[e, dc * 128:(dc + 1) * 128, :]).then_inc(s["mw0"].h, 16)
            for hc in range(NDC):
                g.dma_start(wd_ar(hc), self.w_d[e, hc * 128:(hc + 1) * 128, :]).then_inc(s["mw0"].h, 16)
        if not NOW:
            self.em("gpsimd", gps_w)

        last_gu_mm = 0
        for gidx in range(NG if not NOF else 0):
            silu_ts = []
            mult_ts = []
            gu_ts = []
            for hc in range(NDC):
                pg = self.pf[(hc % 2) * 2]
                pu = self.pf[(hc % 2) * 2 + 1]
                mm_t = s["trp"].n + 2; s["trp"].n += 2
                gu_ts.append(mm_t)
                def ten_gu(t, e=e, gidx=gidx, hc=hc, pg=pg, pu=pu, wxe=xe_ready, wld=wld,
                           wfree=mult_ts[hc - 2] if hc >= 2 else prev_group_pf, wcnt=self.cnt_idx_t):
                    t.wait_ge(s["idx"].h, wcnt)
                    t.wait_ge(s["moe"].h, wxe)
                    if wld:
                        t.wait_ge(s["mw0"].h, wld)
                    if wfree:
                        t.wait_ge(s["moe"].h, wfree)
                    rc = t.to_reg(t.value_load(self.cnt_i32[0:1, e:e + 1]))
                    with t.If_cmp(rc, 512 * gidx, "IS_GT"):
                        for dc in range(NDC):
                            mm = t.matmul(pg[:, :], wg_ar(dc)[:, hc * 128:(hc + 1) * 128],
                                          self.xe_sb[:, dc, gidx * 512:(gidx + 1) * 512],
                                          start=(dc == 0), stop=(dc == NDC - 1))
                            if dc == NDC - 1:
                                mm.then_inc(s["trp"].h, 1)
                        for dc in range(NDC):
                            mm = t.matmul(pu[:, :], wu_ar(dc)[:, hc * 128:(hc + 1) * 128],
                                          self.xe_sb[:, dc, gidx * 512:(gidx + 1) * 512],
                                          start=(dc == 0), stop=(dc == NDC - 1))
                            if dc == NDC - 1:
                                mm.then_inc(s["trp"].h, 1)
                    with t.Else():
                        t.sem_inc(s["trp"].h, 2)
                    t.end_ifs()
                    t.free_register(rc)
                self.em("tensor", ten_gu)
                sl_t = s["moe"].n + 1; s["moe"].n += 1
                silu_ts.append(sl_t)
                def sca_s(sc, e=e, gidx=gidx, pg=pg, w=mm_t, wprev=last_ml):
                    sc.wait_ge(s["trp"].h, w)
                    if wprev:
                        sc.wait_ge(s["moe"].h, wprev)
                    rc = sc.to_reg(sc.value_load(self.cnt_i32[0:1, e:e + 1]))
                    with sc.If_cmp(rc, 512 * gidx, "IS_GT"):
                        sc.activation(self.sil_sb[:, :], pg[:, :], AF.Silu).then_inc(s["moe"].h, 1)
                    with sc.Else():
                        sc.sem_inc(s["moe"].h, 1)
                    sc.end_ifs()
                    sc.free_register(rc)
                self.em("scalar", sca_s)
                ml_t = s["moe"].n + 1; s["moe"].n += 1
                mult_ts.append(ml_t)
                def vec_m(v, e=e, gidx=gidx, hc=hc, pu=pu, w=sl_t, wmm=mm_t):
                    v.wait_ge(s["moe"].h, w)
                    v.wait_ge(s["trp"].h, wmm)
                    rc = v.to_reg(v.value_load(self.cnt_i32[0:1, e:e + 1]))
                    with v.If_cmp(rc, 512 * gidx, "IS_GT"):
                        v.tensor_tensor(self.hid_sb[:, hc, :], self.sil_sb[:, :], pu[:, :], op=ALU.mult).then_inc(s["moe"].h, 1)
                    with v.Else():
                        v.sem_inc(s["moe"].h, 1)
                    v.end_ifs()
                    v.free_register(rc)
                self.em("vector", vec_m)
                last_ml = ml_t
            hid_done = s["moe"].n
            last_gu_mm = gu_ts[-1] + 1

            for j in range(4):
                mm_d = s["trp"].n + 2; s["trp"].n += 2
                def ten_d(t, e=e, gidx=gidx, j=j, whid=hid_done, wev=s["moe"].n):
                    t.wait_ge(s["moe"].h, max(whid, wev))
                    rc = t.to_reg(t.value_load(self.cnt_i32[0:1, e:e + 1]))
                    with t.If_cmp(rc, 512 * gidx, "IS_GT"):
                        for half in range(2):
                            for hc in range(NDC):
                                mm = t.matmul(self.pf[4 + half][:, :],
                                              self.hid_sb[:, hc, j * 128:(j + 1) * 128],
                                              wd_ar(hc)[:, half * 512:(half + 1) * 512],
                                              start=(hc == 0), stop=(hc == NDC - 1))
                                if hc == NDC - 1:
                                    mm.then_inc(s["trp"].h, 1)
                    with t.Else():
                        t.sem_inc(s["trp"].h, 2)
                    t.end_ifs()
                    t.free_register(rc)
                self.em("tensor", ten_d)
                ev_t = s["moe"].n + 1; s["moe"].n += 1
                dnp = "dn" if j % 2 == 0 else "dn1"
                def sca_e(sc, e=e, gidx=gidx, j=j, w=mm_d, dnp=dnp, wsc=self.dn_par[j % 2]):
                    sc.wait_ge(s["trp"].h, w)
                    rc = sc.to_reg(sc.value_load(self.cnt_i32[0:1, e:e + 1]))
                    if wsc:
                        sc.wait_ge(s[dnp].h, wsc)
                    with sc.If_cmp(rc, 512 * gidx, "IS_GT"):
                        sc.copy(self.down_sb[j % 2][:, 0:512], self.pf[4][:, :]).then_inc(s["moe"].h, 1)
                    with sc.Else():
                        sc.sem_inc(s["moe"].h, 1)
                    sc.end_ifs()
                    sc.free_register(rc)
                self.em("scalar", sca_e)
                ev2_t = s["moe"].n + 1; s["moe"].n += 1
                def vec_e(v, e=e, gidx=gidx, j=j, w=mm_d, dnp=dnp, wsc=self.dn_par[j % 2]):
                    v.wait_ge(s["trp"].h, w)
                    rc = v.to_reg(v.value_load(self.cnt_i32[0:1, e:e + 1]))
                    if wsc:
                        v.wait_ge(s[dnp].h, wsc)
                    with v.If_cmp(rc, 512 * gidx, "IS_GT"):
                        v.tensor_copy(self.down_sb[j % 2][:, 512:1024], self.pf[5][:, :]).then_inc(s["moe"].h, 1)
                    with v.Else():
                        v.sem_inc(s["moe"].h, 1)
                    v.end_ifs()
                    v.free_register(rc)
                self.em("vector", vec_e)
                s[dnp].n += 16
                self.dn_par[j % 2] = s[dnp].n
                row0 = e * 2048 + 512 * gidx + 128 * j
                def sca_d(sc, e=e, gidx=gidx, j=j, dnp=dnp, row0=row0, wev2=ev2_t):
                    sc.wait_ge(s["moe"].h, wev2)
                    rc = sc.to_reg(sc.value_load(self.cnt_i32[0:1, e:e + 1]))
                    with sc.If_cmp(rc, 512 * gidx, "IS_GT"):
                        sc.dma_start(self.down_hbm[row0:row0 + 128, :],
                                     self.down_sb[j % 2][:, :]).then_inc(s[dnp].h, 16)
                    with sc.Else():
                        sc.sem_inc(s[dnp].h, 16)
                    sc.end_ifs()
                    sc.free_register(rc)
                self.em("scalar", sca_d)
            prev_group_pf = s["moe"].n
        xe_users_done = last_gu_mm if not NOF else s["xe"].n and 0
        w_users_done = s["trp"].n
    self.moe_done_dn = s["dn"].n
    self.moe_done_dn1 = s["dn1"].n
    self.moe_done_trp = s["trp"].n


def _ph_combine(self):
    s = self.s
    prev_y = [0, 0]
    for c in range(NCHUNK):
        for k in range(2):
            par = (2 * c + k) % 2
            cgp = "cg" if par == 0 else "cg1"
            s[cgp].n += 16
            def gps(g, c=c, k=k, par=par, cgp=cgp, wdn=self.moe_done_dn, wdn1=self.moe_done_dn1,
                    widx=self.sl_ready, wy=prev_y[par]):
                from concourse.bass import IndirectOffsetOnAxis
                g.wait_ge(s["dn"].h, wdn)
                g.wait_ge(s["dn1"].h, wdn1)
                g.wait_ge(s["idx"].h, widx)
                if wy:
                    g.wait_ge(s["y"].h, wy)
                g.indirect_dma_start(
                    self.gbuf[par][:, :],
                    None,
                    self.down_hbm[:, :],
                    IndirectOffsetOnAxis(ap=self.cg_i32[k][:, c:c + 1], axis=0),
                ).then_inc(s[cgp].h, 16)
            self.em("gpsimd", gps)
            gw = s[cgp].n
            s["y"].n += 1
            prev_y[par] = s["y"].n
            def vec(v, c=c, k=k, par=par, cgp=cgp, w=gw, wy=self.y_attn, wwk=self.wk_ready):
                v.wait_ge(s[cgp].h, w)
                v.wait_ge(s["y"].h, wy)
                v.wait_ge(s["idx"].h, wwk)
                v.drain()
                v.scalar_tensor_tensor(self.xt[c][:, :], self.gbuf[par][:, :], self.wk[k][:, c:c + 1],
                                       self.xt[c][:, :], op0=ALU.mult, op1=ALU.add).then_inc(s["y"].h, 1)
            self.em("vector", vec)
    self.y_final = s["y"].n


def _ph_final(self):
    s = self.s
    for c in range(NCHUNK):
        par = c % 2
        fp = "fin" if par == 0 else "fin1"
        sx = s[f"xr{par}"]
        sx.n += 16
        xw = sx.n
        def syncx(sy, c=c, par=par, fp=fp, wfin=0 if c < 2 else None,
                  wdn=self.moe_done_dn, wdn1=self.moe_done_dn1, wfv=s[fp].n):
            sy.wait_ge(s["dn"].h, wdn)
            sy.wait_ge(s["dn1"].h, wdn1)
            if wfv > 0:
                sy.wait_ge(s[fp].h, wfv)
            sy.dma_start(self.down_sb[par][:, :], self.x[c * 128:(c + 1) * 128, :]).then_inc(s[f"xr{par}"].h, 16)
        self.em("sync", syncx)
        ad_t = s["y"].n + 1; s["y"].n += 1
        def vec(v, c=c, par=par, wx=xw, wy=self.y_final):
            v.wait_ge(s[f"xr{par}"].h, wx)
            v.wait_ge(s["y"].h, wy)
            v.tensor_tensor(self.down_sb[par][:, :], self.down_sb[par][:, :], self.xt[c][:, :], op=ALU.add).then_inc(s["y"].h, 1)
        self.em("vector", vec)
        s[fp].n += 16
        def synco(sy, c=c, par=par, fp=fp, w=ad_t):
            sy.wait_ge(s["y"].h, w)
            sy.dma_start(self.out[c * 128:(c + 1) * 128, :], self.down_sb[par][:, :]).then_inc(s[fp].h, 16)
        self.em("sync", synco)
    fin_all = s["fin"].n
    fin1_all = s["fin1"].n
    def syncf(sy, w=fin_all, w1=fin1_all, dw=s["dbg"].n):
        sy.wait_ge(s["fin"].h, w)
        sy.wait_ge(s["fin1"].h, w1)
        if dw:
            sy.wait_ge(s["dbg"].h, dw)
    self.em("sync", syncf)


Builder.ph_attn_prep = _ph_attn_prep
Builder.ph_attention = _ph_attention
Builder.attn_cat_view = _attn_cat_view
Builder.ph_woT = _ph_woT
Builder.ph_outproj = _ph_outproj
Builder.finish_attn_dbg = _finish_attn_dbg
Builder.ph_moe = _ph_moe
Builder.ph_combine = _ph_combine
Builder.ph_final = _ph_final

def host_constants():
    import ml_dtypes
    ident = np.eye(128, dtype=np.float32)
    tri = (np.arange(128)[:, None] < np.arange(128)[None, :]).astype(np.float32)
    iota16 = np.tile(np.arange(E, dtype=np.float32), (128, 1))
    masters = np.zeros((H, 128, 4096), dtype=np.float32)
    slopes = [2 ** (-0.5 * (i + 1)) for i in range(H)]
    j = np.arange(128)[:, None]
    u = np.arange(4096)[None, :]
    for h in range(H):
        rel = (u - 1920 - j).astype(np.float32)
        masters[h] = 8.0 * np.minimum(rel * slopes[h], 8.0)
    identidx = np.zeros((128, 128), dtype=np.float32)
    ii = np.arange(CBLK)
    identidx[ii % 16, ii // 16] = ii
    return {
        "identidx_c": identidx,
        "ident": ident.astype(ml_dtypes.bfloat16),
        "identf": ident,
        "tri": tri.astype(ml_dtypes.bfloat16),
        "iota16": iota16,
        "iota128": np.arange(128, dtype=np.float32)[:, None],
        "masters": masters.astype(ml_dtypes.bfloat16),
    }


def core_inputs(core_id, x_core, w):
    d = dict(host_constants())
    d.update({
        "x": np.ascontiguousarray(x_core),
        "w_in": w["w_in"], "b_in": w["b_in"].reshape(-1, 1),
        "w_out": w["w_out"], "b_out": w["b_out"].reshape(-1, 1),
        "w_r": w["w_r"], "w_g": w["w_g"], "w_u": w["w_u"], "w_d": w["w_d"],
        "rankmask": (np.arange(8) < core_id).astype(np.float32)[:, None],
    })
    return d


def kernel(**inputs):
    X = np.asarray(inputs["L0_attn_x"])
    w = {
        "w_in": np.asarray(inputs["L0_attn_in_proj_weight"]),
        "b_in": np.asarray(inputs["L0_attn_in_proj_bias"]),
        "w_out": np.asarray(inputs["L0_attn_out_proj_weight"]),
        "b_out": np.asarray(inputs["L0_attn_out_proj_bias"]),
        "w_r": np.asarray(inputs["L0_ffn_router_w"]),
        "w_g": np.asarray(inputs["L0_ffn_gate_w"]),
        "w_u": np.asarray(inputs["L0_ffn_up_w"]),
        "w_d": np.asarray(inputs["L0_ffn_down_w"]),
    }
    b = Builder(stage="full")
    nc = b.build()
    from concourse.bass_utils import run_bass_kernel_spmd
    in_maps = [core_inputs(i, X[i], w) for i in range(8)]
    res = run_bass_kernel_spmd(nc, in_maps, core_ids=list(range(8)))
    out = np.stack([np.asarray(res.results[i]["out"]).reshape(T, D).astype(np.float32)
                    for i in range(8)], 0)
    return out
